# version 02 (final)
# baseline (speedup 1.0000x reference)
import sys

try:
    import concourse.bass as bass
except Exception:
    sys.path.insert(0, "/opt/trn_rl_repo")

import numpy as np
import ml_dtypes
import concourse.bass as bass
import concourse.mybir as mybir
from concourse.bass_utils import run_bass_kernel_spmd

F32 = mybir.dt.float32
I32 = mybir.dt.int32
BF16 = mybir.dt.bfloat16
AF = mybir.ActivationFunctionType
ALU = mybir.AluOpType

B, T, E = 4, 4096, 1024
H, DH = 16, 64
CONTEXT = 250
MAX_PERIOD = 10000.0
NT = 17
CT = NT * 128
ST = [(0, 4), (4, 4), (8, 4), (12, 4), (16, 1)]
EPS = 1e-5
MAGIC = 0x5f3759df
SC_FENCE = True

DEBUG_TAPS = []
EXPFUNC = AF.Exp

ENGINES = ("tensor", "vector", "scalar", "sync", "gpsimd")


class KB:
    def __init__(self):
        self.ops = []
        self.nbar = 0
        self.serial = False

    def op(self, eng, fn, deps=(), tag="", fence=False):
        deps = [d for d in deps if d is not None]
        idx = len(self.ops)
        if self.serial and idx > 0 and self.ops[-1]["kind"] == "op":
            deps = deps + [idx - 1]
        self.ops.append(dict(eng=eng, fn=fn, deps=deps, idx=idx,
                             inc=16 if eng in ("sync", "gpsimd") else 1,
                             kind="op", tag=tag, fence=fence))
        return idx

    def barrier(self):
        idx = len(self.ops)
        self.ops.append(dict(eng=None, fn=None, deps=[], idx=idx, inc=0, kind="bar"))
        self.nbar += 1
        return idx

    def final_wait(self, eng):
        idx = len(self.ops)
        self.ops.append(dict(eng=eng, fn=None, deps=[], idx=idx, inc=0, kind="final"))
        return idx

    def emit(self, nc, sems, sem_bar):
        ops = self.ops
        needed = set()
        for o in ops:
            for d in o["deps"]:
                needed.add(d)
        cum = {e: 0 for e in ENGINES}
        total = {e: 0 for e in ENGINES}
        for o in ops:
            if o["kind"] == "op":
                if o["idx"] in needed or o["eng"] in ("sync", "gpsimd") or o.get("fence"):
                    o["do_inc"] = True
                    cum[o["eng"]] += o["inc"]
                else:
                    o["do_inc"] = False
                o["cum_after"] = cum[o["eng"]]
            elif o["kind"] == "bar":
                o["dma_cum"] = {"sync": cum["sync"], "gpsimd": cum["gpsimd"]}
        for e in ENGINES:
            total[e] = cum[e]

        barcount = [0]

        def body_for(ename):
            def body(eng):
                waited = {}
                nbar_seen = 0
                for o in ops:
                    if o["kind"] == "bar":
                        nbar_seen += 1
                        eng.nop().then_inc(sem_bar, 1)
                        eng.wait_ge(sem_bar, 5 * nbar_seen)
                        for e2 in ("sync", "gpsimd"):
                            v = o["dma_cum"][e2]
                            if v > 0 and waited.get(e2, 0) < v:
                                eng.wait_ge(sems[e2], v)
                                waited[e2] = v
                        continue
                    if o["eng"] != ename:
                        continue
                    if o["kind"] == "final":
                        for e2 in ENGINES:
                            if total[e2] > 0:
                                eng.wait_ge(sems[e2], total[e2])
                        continue
                    wv = {}
                    for d in o["deps"]:
                        dd = ops[d]
                        if dd["eng"] == ename:
                            continue
                        assert dd.get("do_inc"), f"dep {d} not inc'd"
                        e2 = dd["eng"]
                        wv[e2] = max(wv.get(e2, 0), dd["cum_after"])
                    for e2, v in wv.items():
                        if waited.get(e2, 0) < v:
                            eng.wait_ge(sems[e2], v)
                            waited[e2] = v
                    ins = o["fn"](eng)
                    if o["do_inc"]:
                        assert ins is not None, f"op {o['idx']} {o['tag']} returned no instruction"
                        ins.then_inc(sems[ename], o["inc"])
                    if o.get("fence"):
                        eng.wait_ge(sems[ename], o["cum_after"])
                        waited[ename] = max(waited.get(ename, 0), o["cum_after"])
            return body
        return body_for


def _rope_tables(base, dtype=ml_dtypes.bfloat16):
    i = np.arange(32, dtype=np.float64)
    freqs = np.exp(i * (-np.log(MAX_PERIOD) * 2.0 / DH))
    p = np.arange(128)
    f = freqs[p % 32]
    t = base + np.arange(CT, dtype=np.float64)
    ang = f[:, None] * t[None, :]
    return np.cos(ang).astype(dtype), np.sin(ang).astype(dtype)


def _qk_perm():
    n = np.arange(1024)
    dc = n // 128
    g = dc // 2
    plane = dc % 2
    p = n % 128
    m = p // 32
    i = p % 32
    h = 4 * g + m
    return 64 * h + 2 * i + plane


def _mask3():
    sl = np.arange(128)[:, None]
    tl = np.arange(128)[None, :]
    m = np.zeros((128, 384), dtype=ml_dtypes.bfloat16)
    m[:, 0:128] = (tl >= sl).astype(ml_dtypes.bfloat16)
    m[:, 128:256] = ((tl - sl) <= 121).astype(ml_dtypes.bfloat16)
    m[:, 256:384] = ((tl - sl) <= -7).astype(ml_dtypes.bfloat16)
    return m


def host_prep(L0_attn_query, L0_attn_in_proj_w, L0_attn_out_proj_w,
              L0_ffn_W_r, L0_ffn_W_k, L0_ffn_W_v, L0_ffn_mu_r, L0_ffn_mu_k):
    perm = _qk_perm()
    Wq = L0_attn_in_proj_w[0:1024][perm]
    Wk = L0_attn_in_proj_w[1024:2048][perm]
    Wv = L0_attn_in_proj_w[2048:3072]
    w_in_t = np.ascontiguousarray(np.concatenate([Wq, Wk, Wv], axis=0).T)
    w_o_t = np.ascontiguousarray(L0_attn_out_proj_w.T)
    w_r_t = np.ascontiguousarray(L0_ffn_W_r.T)
    w_k_t = np.ascontiguousarray(L0_ffn_W_k.T)
    w_v_t = np.ascontiguousarray(L0_ffn_W_v.T)
    mu = np.empty((128, 16), dtype=np.float32)
    mu[:, 0:8] = L0_ffn_mu_r.reshape(8, 128).T
    mu[:, 8:16] = L0_ffn_mu_k.reshape(8, 128).T
    mask = _mask3()
    ident = np.eye(128, dtype=ml_dtypes.bfloat16)
    cos_e, sin_e = _rope_tables(0)
    cos_o, sin_o = _rope_tables(T - CT)

    shared = dict(w_in_t=w_in_t, w_o_t=w_o_t, w_r_t=w_r_t, w_k_t=w_k_t,
                  w_v_t=w_v_t, mu=mu, mask=np.ascontiguousarray(mask), ident=ident)
    in_maps = []
    for b in range(B):
        for par in range(2):
            if par == 0:
                xs = L0_attn_query[b, 0:CT]
                cos, sin = cos_e, sin_e
            else:
                xs = L0_attn_query[b, T - CT:T]
                cos, sin = cos_o, sin_o
            m = dict(shared)
            m["x"] = np.ascontiguousarray(xs)
            m["cos_t"] = np.ascontiguousarray(cos)
            m["sin_t"] = np.ascontiguousarray(sin)
            in_maps.append(m)
    return in_maps


def build_program(debug=False, no_rope=False, serial_s3=False):
    nc = bass.Bass("TRN2", target_bir_lowering=False)
    P = 128

    x_h = nc.dram_tensor("x", [CT, 1024], F32, kind="ExternalInput").ap()
    w_in_h = nc.dram_tensor("w_in_t", [1024, 3072], F32, kind="ExternalInput").ap()
    w_o_h = nc.dram_tensor("w_o_t", [1024, 1024], F32, kind="ExternalInput").ap()
    w_r_h = nc.dram_tensor("w_r_t", [1024, 1024], F32, kind="ExternalInput").ap()
    w_k_h = nc.dram_tensor("w_k_t", [1024, 1024], F32, kind="ExternalInput").ap()
    w_v_h = nc.dram_tensor("w_v_t", [1024, 1024], F32, kind="ExternalInput").ap()
    mu_h = nc.dram_tensor("mu", [128, 16], F32, kind="ExternalInput").ap()
    mask_h = nc.dram_tensor("mask", [128, 384], BF16, kind="ExternalInput").ap()
    ident_h = nc.dram_tensor("ident", [128, 128], BF16, kind="ExternalInput").ap()
    cos_h = nc.dram_tensor("cos_t", [128, CT], BF16, kind="ExternalInput").ap()
    sin_h = nc.dram_tensor("sin_t", [128, CT], BF16, kind="ExternalInput").ap()
    out_h = nc.dram_tensor("out_y", [CT, 1024], F32, kind="ExternalOutput").ap()

    ARENA = 196608
    base, _arena_end = nc.bump_sbuf(ARENA)

    def sb(name, cols, dtype, off):
        nb = cols * (2 if dtype == BF16 else 4)
        assert off + nb <= ARENA, f"{name} overflows arena: {off}+{nb} > {ARENA}"
        return nc.alloc_sbuf_tensor_at(name, [P, cols], dtype, offset=base + off).ap()

    mask_s = sb("mask_s", 384, BF16, 0)
    ident_s = sb("ident_s", 128, BF16, 768)
    mu_s = sb("mu_s", 16, F32, 1024)
    stats1 = sb("stats1", 68, F32, 1088)
    stats2 = sb("stats2", 68, F32, 1376)
    recip_s = sb("recip_s", 8, F32, 1664)
    tmpa = sb("tmpa", 17, F32, 1696)
    tmpb = sb("tmpb", 17, F32, 1792)
    tmpc = sb("tmpc", 17, F32, 1888)
    r_bf = sb("r_bf", 1024, BF16, 2048)
    attnT_ring = sb("attnT_ring", 2 * 1024, BF16, 4096)

    cos_s = sb("cos_s", CT, BF16, 8192)
    sin_s = sb("sin_s", CT, BF16, 8192 + 4352)
    x_ring = sb("x_ring", 2 * 1024, F32, 8192)
    y_bfb = sb("y_bfb", 4 * 1024, BF16, 8192)

    y_pre = sb("y_pre", NT * 1024, F32, 16896)
    attn_tm = sb("attn_tm", NT * 1024, BF16, 16896 + 34816)
    xT = sb("xT", 8 * CT, BF16, 16896)
    exp_ring = sb("exp_ring", 4 * 4 * 384, BF16, 16896)
    x_bfr = sb("x_bfr", 2 * 1024, BF16, 51712)
    w_ring = sb("w_ring", 4 * 1024, BF16, 55808)
    rope_t = sb("rope_t", 4 * 512, BF16, 64000)
    w_vin = sb("w_vin", 8 * 1024, BF16, 68096)

    qTa = sb("qTa", 4 * CT, BF16, 86528)
    qTb = sb("qTb", 4 * CT, BF16, 103936)
    kTa = sb("kTa", 4 * CT, BF16, 121344)
    kTb = sb("kTb", 4 * CT, BF16, 138752)
    w_o_b = sb("w_o_b", 8 * 1024, BF16, 86528)
    kact = sb("kact", 8 * 512, BF16, 86528)
    xkT = sb("xkT", 8 * 512, BF16, 94720)
    w_k_b = sb("w_k_b", 8 * 1024, BF16, 102912)
    w_r_b = sb("w_r_b", 8 * 1024, BF16, 119296)
    w_v_b = sb("w_v_b", 8 * 1024, BF16, 135680)
    sq_s = sb("sq_s", 1024, BF16, 152064)

    v_sb = sb("v_sb", NT * 1040, BF16, 156160)
    yT = sb("yT", 8 * 513, BF16, 156160)
    xrT = sb("xrT", 8 * 512, BF16, 164384)
    ytmp = sb("ytmp", 2 * 1024, F32, 172576)
    ffn_s = sb("ffn_s", 1024, F32, 180768)
    d1_s = sb("d1_s", 512, BF16, 184864)
    tmp_bf = sb("tmp_bf", 2 * 512, BF16, 185888)
    tmpd = sb("tmpd", 17, F32, 191520)
    tmpe = sb("tmpe", 17, F32, 191616)
    magic_s = sb("magic_s", 17, I32, 191712)
    dbgstage = sb("dbgstage", 260, F32, 191808)
    flush_s = sb("flush_s", 32, BF16, 192864)

    kb = KB()
    OP = kb.op

    def dump(name, ap):
        if not debug:
            return
        dh = nc.dram_tensor("dbg_" + name, list(ap.shape), ap.dtype, kind="ExternalOutput").ap()
        OP("sync", lambda e, dh=dh, ap=ap: e.dma_start(out=dh, in_=ap), tag="dump_" + name)
        kb.barrier()

    def GDMA(fn, deps=(), tag=""):
        OP("gpsimd", fn, deps=deps, tag=tag)
        return OP("gpsimd", lambda e: e.dma_start(out=flush_s[:, 16:32], in_=flush_s[:, 0:16]),
                  deps=[], tag=tag + ".fl")

    def emit_rsqrt(xa, ya, ta, dep0, tag):
        yi = ya.bitcast(I32)
        xi = xa.bitcast(I32)
        n = xa.shape[-1]
        mg = magic_s[:, 0:n]
        OP("vector", lambda e, yi=yi, xi=xi: e.tensor_scalar(yi, xi, 1, None, ALU.arith_shift_right, ALU.bypass),
           deps=[dep0] if dep0 is not None else [], tag=tag + ".sd", fence=True)
        OP("vector", lambda e, yi=yi, mg=mg: e.tensor_sub(yi, mg, yi),
           deps=[], tag=tag + ".sb", fence=True)
        last = None
        for it_ in range(2):
            OP("vector", lambda e, ta=ta, ya=ya: e.tensor_mul(ta, ya, ya),
               deps=[], tag=tag + f".a{it_}", fence=True)
            OP("vector", lambda e, ta=ta, xa=xa: e.tensor_mul(ta, xa, ta),
               deps=[], tag=tag + f".b{it_}", fence=True)
            OP("vector", lambda e, ta=ta: e.tensor_scalar(ta, ta, -0.5, 1.5, ALU.mult, ALU.add),
               deps=[], tag=tag + f".c{it_}", fence=True)
            last = OP("vector", lambda e, ya=ya, ta=ta: e.tensor_mul(ya, ya, ta),
                      deps=[], tag=tag + f".d{it_}", fence=True)
        return last

    def xT_c(c):
        return xT[:, c * CT:(c + 1) * CT]

    qk_t = {(0, 0): qTa, (0, 1): qTb, (1, 0): kTa, (1, 1): kTb}

    def ring_slot(h, s):
        off = (h * 4 + s) * 384
        return exp_ring[:, off:off + 384]

    t_mask = OP("sync", lambda e: e.dma_start(out=mask_s[:, :], in_=mask_h), tag="ld_mask")
    t_ident = OP("sync", lambda e: e.dma_start(out=ident_s[:, :], in_=ident_h), tag="ld_ident")
    t_mu = OP("sync", lambda e: e.dma_start(out=mu_s[:, :], in_=mu_h), tag="ld_mu")
    t_cos = OP("sync", lambda e: e.dma_start(out=cos_s[:, :], in_=cos_h), tag="ld_cos")
    t_sin = OP("sync", lambda e: e.dma_start(out=sin_s[:, :], in_=sin_h), tag="ld_sin")
    OP("vector", lambda e: e.memset(magic_s[:, :], MAGIC), tag="magic")
    OP("scalar", lambda e: e.preload_act_table(AF.Exp), tag="pre_exp")
    OP("scalar", lambda e: e.activation(tmpd[:, 0:1], mu_s[:, 0:1], AF.Exp),
       deps=[t_mu], tag="warm_exp")

    with (
        nc.psum_tensor("ps_qk", [P, 2 * 512], F32) as ps_qk_h,
        nc.psum_tensor("ps_v", [P, 2 * 1024], F32) as ps_v_h,
        nc.psum_tensor("ps_tr", [P, 2 * 1024], BF16) as ps_tr_h,
    ):
        ps_tr = ps_tr_h.ap()
        ps_qk = ps_qk_h.ap()
        ps_v = ps_v_h.ap()

        tok_xd = [None] * NT
        tok_tr = [None] * NT
        tok_xte = [None] * NT
        for t in range(NT):
            slot = t % 2

            def f_xd(e, t=t, slot=slot):
                return e.dma_start(out=x_bfr[:, slot * 1024:(slot + 1) * 1024],
                                   in_=x_h[t * 128:(t + 1) * 128, :])
            tok_xd[t] = GDMA(f_xd, deps=[tok_tr[t - 2] if t >= 2 else None], tag=f"xdma{t}")

            def f_tr(e, t=t, slot=slot):
                ins = None
                for c in range(8):
                    ins = e.transpose(ps_tr[:, slot * 1024 + c * 128: slot * 1024 + (c + 1) * 128],
                                      x_bfr[:, slot * 1024 + c * 128: slot * 1024 + (c + 1) * 128],
                                      ident_s[:, :])
                return ins
            tok_tr[t] = OP("tensor", f_tr,
                           deps=[tok_xd[t], t_ident, tok_xte[t - 2] if t >= 2 else None],
                           tag=f"xtr{t}")

            def f_xte(e, t=t, slot=slot):
                dst = xT.rearrange("p (c w) -> p c w", c=8)[:, :, t * 128:(t + 1) * 128]
                src = ps_tr[:, slot * 1024:(slot + 1) * 1024].rearrange("p (c w) -> p c w", c=8)
                return e.tensor_copy(dst, src)
            tok_xte[t] = OP("vector", f_xte, deps=[tok_tr[t]], tag=f"xte{t}")

        tok_wd = [None] * 16
        tok_mm = [[None] * len(ST) for _ in range(16)]
        tok_ev = [[None] * len(ST) for _ in range(16)]
        tok_rope = {}

        for dc in range(16):
            slot = dc % 4

            def f_wd(e, dc=dc, slot=slot):
                src = w_in_h.rearrange("(cc p) d -> p cc d", p=128)[:, :, dc * 128:(dc + 1) * 128]
                dst = w_ring[:, slot * 1024:(slot + 1) * 1024].rearrange("p (cc w) -> p cc w", cc=8)
                return e.dma_start(out=dst, in_=src)
            dep_free = tok_mm[dc - 4][len(ST) - 1] if dc >= 4 else None
            assert dc < 4 or dep_free is not None
            tok_wd[dc] = GDMA(f_wd, deps=[dep_free], tag=f"wd{dc}")
            for si, (t0, ntl) in enumerate(ST):
                k = dc * len(ST) + si
                pslot = k % 2
                ncl = ntl * 128

                def f_mm(e, dc=dc, slot=slot, t0=t0, ncl=ncl, pslot=pslot):
                    ins = None
                    for cc in range(8):
                        lhsT = w_ring[:, slot * 1024 + cc * 128: slot * 1024 + (cc + 1) * 128]
                        rhs = xT_c(cc)[:, t0 * 128: t0 * 128 + ncl]
                        ins = e.matmul(ps_qk[:, pslot * 512: pslot * 512 + ncl], lhsT, rhs,
                                       start=(cc == 0), stop=(cc == 7))
                    return ins
                prev_ev = kb.ops[k - 2]["idx"] if False else (tok_ev[(k - 2) // len(ST)][(k - 2) % len(ST)] if k >= 2 else None)
                tok_mm[dc][si] = OP("tensor", f_mm,
                                    deps=[tok_wd[dc], tok_xte[min(t0 + ntl - 1, NT - 1)], prev_ev],
                                    tag=f"qkmm{dc}.{si}")

                plane = dc % 2
                g = (dc // 2) % 4
                isq = 0 if dc < 8 else 1
                dstt = qk_t[(isq, plane)]

                def f_ev(e, dstt=dstt, g=g, t0=t0, ncl=ncl, pslot=pslot):
                    return e.copy(dstt[:, g * CT + t0 * 128: g * CT + t0 * 128 + ncl],
                                  ps_qk[:, pslot * 512: pslot * 512 + ncl])
                tok_ev[dc][si] = OP("scalar", f_ev, deps=[tok_mm[dc][si]], tag=f"qkev{dc}.{si}")

                if plane == 1 and not no_rope:
                    ta = qk_t[(isq, 0)]
                    tb = qk_t[(isq, 1)]

                    def f_rope(e, ta=ta, tb=tb, g=g, t0=t0, ncl=ncl):
                        c0 = g * CT + t0 * 128
                        a = ta[:, c0:c0 + ncl]
                        b = tb[:, c0:c0 + ncl]
                        cs = cos_s[:, t0 * 128: t0 * 128 + ncl]
                        sn = sin_s[:, t0 * 128: t0 * 128 + ncl]
                        t1 = rope_t[:, 0:ncl]
                        t2 = rope_t[:, 512:512 + ncl]
                        t3 = rope_t[:, 1024:1024 + ncl]
                        t4 = rope_t[:, 1536:1536 + ncl]
                        e.tensor_mul(t1, a, cs)
                        e.tensor_mul(t2, a, sn)
                        e.tensor_mul(t3, b, sn)
                        e.tensor_mul(t4, b, cs)
                        e.tensor_sub(a, t1, t3)
                        return e.tensor_add(b, t2, t4)
                    tok_rope[(isq, g, si)] = OP(
                        "vector", f_rope,
                        deps=[tok_ev[dc - 1][si], tok_ev[dc][si], t_cos, t_sin],
                        tag=f"rope{isq}.{g}.{si}")

        tok_wv = [None] * 8
        for cc in range(8):
            def f_wv(e, cc=cc):
                src = w_in_h[cc * 128:(cc + 1) * 128, 2048:3072]
                return e.dma_start(out=w_vin[:, cc * 1024:(cc + 1) * 1024], in_=src)
            tok_wv[cc] = GDMA(f_wv, tag=f"wv{cc}")

        def f_ones(e):
            v = v_sb.rearrange("p (t h c) -> p t h c", h=16, c=65)[:, :, :, 64]
            return e.memset(v, 1.0)
        tok_ones = OP("vector", f_ones, tag="vones")

        tok_vm = [None] * NT
        tok_vev = [None] * NT
        for t in range(NT):
            pslot = t % 2

            def f_vm(e, t=t, pslot=pslot):
                ins = None
                for cc in range(8):
                    lhsT = xT_c(cc)[:, t * 128:(t + 1) * 128]
                    for half in range(2):
                        ins = e.matmul(ps_v[:, pslot * 1024 + half * 512: pslot * 1024 + (half + 1) * 512],
                                       lhsT,
                                       w_vin[:, cc * 1024 + half * 512: cc * 1024 + (half + 1) * 512],
                                       start=(cc == 0), stop=(cc == 7))
                return ins
            tok_vm[t] = OP("tensor", f_vm,
                           deps=tok_wv + [tok_xte[t], tok_vev[t - 2] if t >= 2 else None],
                           tag=f"vmm{t}")

            def f_vev(e, t=t, pslot=pslot):
                dst = v_sb[:, t * 1040:(t + 1) * 1040].rearrange("p (h c) -> p h c", c=65)[:, :, 0:64]
                src = ps_v[:, pslot * 1024:(pslot + 1) * 1024].rearrange("p (h c) -> p h c", c=64)
                return e.copy(dst, src)
            tok_vev[t] = OP("scalar", f_vev, deps=[tok_vm[t], tok_ones], tag=f"vev{t}")

    kb.barrier()
    for nm, ap_ in (("xT", xT), ("qTa", qTa), ("qTb", qTb), ("kTa", kTa),
                    ("kTb", kTb), ("v_sb", v_sb)):
        dump(nm, ap_[:, :])

    kb.serial = serial_s3
    with (
        nc.psum_tensor("ps_sc", [P, 4 * 512], F32) as ps_sc_h,
        nc.psum_tensor("ps_av", [P, 2 * 512], F32) as ps_av_h,
        nc.psum_tensor("ps_plug", [P, 512], F32) as ps_plug_h,
    ):
        ps_sc = ps_sc_h.ap()
        ps_av = ps_av_h.ap()
        ps_plug = ps_plug_h.ap()

        prev_ae_15 = None
        prev_ae_16 = None
        prev_ex_16 = None
        for sw in range(4):
            tok_sc = [None] * NT
            tok_ex = [None] * NT
            tok_mk = [None] * NT
            tok_av = [None] * NT
            tok_ae = [None] * NT
            tok_rc_arr = [None] * NT
            for i in range(NT + 1):
                if i < NT:
                    ncl = 128 * min(3, NT - i)

                    scdep = tok_ex[i - 1] if i >= 1 else prev_ex_16
                    for m in range(4):
                        def f_sc1(e, sw=sw, i=i, ncl=ncl, m=m):
                            band = slice(32 * m, 32 * m + 32)
                            c0 = sw * CT + i * 128
                            e.matmul(ps_sc[:, 512 * m: 512 * m + ncl],
                                     kTa[band, c0:c0 + 128],
                                     qTa[band, c0:c0 + ncl],
                                     start=True, stop=False, tile_position=(32 * m, 0))
                            return e.matmul(ps_sc[:, 512 * m: 512 * m + ncl],
                                            kTb[band, c0:c0 + 128],
                                            qTb[band, c0:c0 + ncl],
                                            start=False, stop=True, tile_position=(32 * m, 0))
                        tok_sc[i] = OP("tensor", f_sc1, deps=[scdep] if m == 0 else [],
                                       tag=f"sc{sw}.{i}.{m}", fence=SC_FENCE)
                    if i >= 2 and tok_rc_arr[i - 2] is not None:
                        kb.ops[tok_rc_arr[i - 2]]["deps"].append(tok_sc[i])
                    if i == 0:
                        def f_plug(e, sw=sw):
                            return e.matmul(ps_plug[:, 0:1], kTa[0:32, 0:128],
                                            qTa[0:32, 0:1], start=True, stop=True,
                                            tile_position=(0, 0))
                        tok_plug0 = OP("tensor", f_plug, deps=[], tag=f"plug{sw}")

                    def f_ex(e, i=i, ncl=ncl):
                        ins = None
                        for h in range(4):
                            ins = e.activation(ring_slot(h, i % 4)[:, 0:ncl],
                                               ps_sc[:, 512 * h: 512 * h + ncl],
                                               EXPFUNC, scale=0.125)
                        return ins
                    exdeps = [tok_sc[i], tok_av[i - 1] if i >= 1 else tok_plug0]
                    tok_ex[i] = OP("scalar", f_ex, deps=exdeps, tag=f"ex{sw}.{i}")
                    tok_exp_plug = OP("scalar", lambda e: e.copy(dbgstage[:, :], dbgstage[:, :]),
                                      deps=[], tag=f"explug{sw}.{i}")

                    def f_mk(e, i=i, ncl=ncl):
                        v = exp_ring.rearrange("p (h s c) -> p h s c", h=4, s=4)[:, :, i % 4, 0:ncl]
                        mk = mask_s[:, 0:ncl].unsqueeze(1).broadcast_to([P, 4, ncl])
                        return e.tensor_tensor(v, v, mk, ALU.mult)
                    tok_mk[i] = OP("vector", f_mk, deps=[tok_exp_plug, t_mask], tag=f"mk{sw}.{i}")
                    if debug and sw == 0 and i <= 2:
                        dh = nc.dram_tensor(f"dbg_ring{i}", [P, 4, 384], BF16,
                                            kind="ExternalOutput").ap()
                        OP("sync", lambda e, dh=dh, i=i: e.dma_start(
                            out=dh, in_=exp_ring.rearrange("p (h s c) -> p h s c", h=4, s=4)[:, :, i % 4, :]),
                           deps=[tok_mk[i]], tag=f"dbgring{i}")


                j = i - 1
                if j >= 0:
                    pslot = j % 2
                    deltas = [d for d in (2, 1, 0) if j - d >= 0]
                    prev_av_j = j

                    def f_av(e, sw=sw, j=j, pslot=pslot, deltas=deltas):
                        ins = None
                        for h in range(4):
                            hg = 4 * sw + h
                            for di, d in enumerate(deltas):
                                lhsT = ring_slot(h, (j - d) % 4)[:, d * 128:(d + 1) * 128]
                                rhs = v_sb[:, (j - d) * 1040 + 65 * hg: (j - d) * 1040 + 65 * hg + 65]
                                ins = e.matmul(ps_av[:, pslot * 512 + 65 * h: pslot * 512 + 65 * h + 65],
                                               lhsT, rhs,
                                               start=(di == 0), stop=(di == len(deltas) - 1))
                        return ins
                    avdeps = [tok_mk[j]]
                    avdeps.append(tok_mk[min(j + 1, NT - 1)])
                    if j >= 1:
                        avdeps.append(tok_ae[j - 1])
                    if j >= 2:
                        avdeps.append(tok_ae[j - 2])
                    else:
                        avdeps.append(prev_ae_16 if j == 0 else prev_ae_15)
                    tok_av[j] = OP("tensor", f_av, deps=avdeps, tag=f"av{sw}.{j}")

                    tok_avslack = None
                    if j >= NT - 2:
                        def f_plugb(e):
                            return e.matmul(ps_plug[:, 0:1], kTa[0:32, 0:128],
                                            qTa[0:32, 0:1], start=True, stop=True,
                                            tile_position=(0, 0))
                        tok_avslack = OP("tensor", f_plugb, deps=[], tag=f"plugb{sw}.{j}")

                    def f_rc(e, pslot=pslot):
                        src = ps_av[:, pslot * 512: pslot * 512 + 260].rearrange(
                            "p (h c) -> p h c", c=65)[:, :, 64]
                        return e.reciprocal(recip_s[:, 0:4], src)
                    rcdeps = [tok_av[j]] if tok_avslack is None else [tok_av[j], tok_avslack]
                    tok_rc = OP("vector", f_rc, deps=rcdeps, tag=f"rc{sw}.{j}", fence=True)
                    tok_rc_arr[j] = tok_rc

                    def f_ae(e, sw=sw, j=j, pslot=pslot):
                        src = ps_av[:, pslot * 512: pslot * 512 + 260].rearrange(
                            "p (h c) -> p h c", c=65)[:, :, 0:64]
                        rc = recip_s[:, 0:4].unsqueeze(2).broadcast_to([P, 4, 64])
                        dst = attn_tm[:, j * 1024 + 256 * sw: j * 1024 + 256 * sw + 256].rearrange(
                            "p (h c) -> p h c", c=64)
                        return e.tensor_tensor(dst, src, rc, ALU.mult)
                    if debug and sw == 0 and j <= 1:
                        dh3 = nc.dram_tensor(f"dbg_av{j}", [P, 260], F32, kind="ExternalOutput").ap()
                        tcp = OP("vector", lambda e, pslot=pslot: e.tensor_copy(
                            dbgstage[:, :], ps_av[:, pslot * 512: pslot * 512 + 260]),
                            deps=[tok_av[j]], tag=f"dbgavcp{j}")
                        OP("sync", lambda e, dh3=dh3: e.dma_start(out=dh3, in_=dbgstage[:, :]),
                           deps=[tcp], tag=f"dbgav{j}")
                    tok_ae[j] = OP("vector", f_ae, deps=[tok_av[j]], tag=f"ae{sw}.{j}")
            prev_ae_15 = tok_ae[15]
            prev_ae_16 = tok_ae[16]
            prev_ex_16 = tok_ex[16]

    kb.serial = False
    kb.barrier()
    dump("attn_tm", attn_tm[:, :])

    with (
        nc.psum_tensor("ps_o", [P, 2 * 1024], F32) as ps_o_h,
        nc.psum_tensor("ps_tr2", [P, 2 * 1024], BF16) as ps_tr2_h,
    ):
        ps_tr2 = ps_tr2_h.ap()
        ps_o = ps_o_h.ap()

        tok_wo = [None] * 8
        tok_wk = [None] * 8
        tok_wr = [None] * 8
        tok_wv2 = [None] * 8
        for cc in range(8):
            def f_wo(e, cc=cc):
                return e.dma_start(out=w_o_b[:, cc * 1024:(cc + 1) * 1024],
                                   in_=w_o_h[cc * 128:(cc + 1) * 128, :])
            tok_wo[cc] = GDMA(f_wo, tag=f"wo{cc}")
        for cc in range(8):
            def f_wk(e, cc=cc):
                return e.dma_start(out=w_k_b[:, cc * 1024:(cc + 1) * 1024],
                                   in_=w_k_h[cc * 128:(cc + 1) * 128, :])
            tok_wk[cc] = GDMA(f_wk, tag=f"wk{cc}")
        for cc in range(8):
            def f_wr(e, cc=cc):
                return e.dma_start(out=w_r_b[:, cc * 1024:(cc + 1) * 1024],
                                   in_=w_r_h[cc * 128:(cc + 1) * 128, :])
            tok_wr[cc] = GDMA(f_wr, tag=f"wr{cc}")
        for cc in range(8):
            def f_wv2(e, cc=cc):
                return e.dma_start(out=w_v_b[:, cc * 1024:(cc + 1) * 1024],
                                   in_=w_v_h[cc * 128:(cc + 1) * 128, :])
            tok_wv2[cc] = GDMA(f_wv2, tag=f"wv2_{cc}")

        tok_xd2 = [None] * NT
        tok_trx = [None] * NT
        tok_evx = [None] * NT
        tok_opm = [None] * NT
        tok_rs1 = [None] * NT
        tok_sq1 = [None] * NT
        for t in range(NT):
            slot = t % 2

            def f_xd2(e, t=t, slot=slot):
                return e.dma_start(out=x_ring[:, slot * 1024:(slot + 1) * 1024],
                                   in_=x_h[t * 128:(t + 1) * 128, :])
            tok_xd2[t] = OP("sync", f_xd2, deps=[tok_rs1[t - 2] if t >= 2 else None],
                            tag=f"xd2_{t}")

            def f_trx(e, t=t, slot=slot):
                ins = None
                for c in range(8):
                    ins = e.transpose(ps_tr2[:, slot * 1024 + c * 128: slot * 1024 + (c + 1) * 128],
                                      attn_tm[:, t * 1024 + c * 128: t * 1024 + (c + 1) * 128],
                                      ident_s[:, :])
                return ins
            tok_trx[t] = OP("tensor", f_trx,
                            deps=[tok_evx[t - 2] if t >= 2 else None],
                            tag=f"trx{t}")

            def f_evx(e, t=t, slot=slot):
                return e.tensor_copy(attnT_ring[:, slot * 1024:(slot + 1) * 1024],
                                     ps_tr2[:, slot * 1024:(slot + 1) * 1024])
            tok_evx[t] = OP("vector", f_evx, deps=[tok_trx[t]], tag=f"evx{t}")

            def f_opm(e, t=t, slot=slot):
                ins = None
                for cc in range(8):
                    lhsT = attnT_ring[:, slot * 1024 + cc * 128: slot * 1024 + (cc + 1) * 128]
                    for half in range(2):
                        ins = e.matmul(ps_o[:, slot * 1024 + half * 512: slot * 1024 + (half + 1) * 512],
                                       lhsT,
                                       w_o_b[:, cc * 1024 + half * 512: cc * 1024 + (half + 1) * 512],
                                       start=(cc == 0), stop=(cc == 7))
                return ins
            tok_opm[t] = OP("tensor", f_opm,
                            deps=tok_wo + [tok_evx[t], tok_rs1[t - 2] if t >= 2 else None],
                            tag=f"opm{t}")

            def f_rs1(e, t=t, slot=slot):
                return e.tensor_add(y_pre[:, t * 1024:(t + 1) * 1024],
                                    ps_o[:, slot * 1024:(slot + 1) * 1024],
                                    x_ring[:, slot * 1024:(slot + 1) * 1024])
            tok_rs1[t] = OP("vector", f_rs1, deps=[tok_opm[t], tok_xd2[t]], tag=f"rs1_{t}")

            def f_sum1(e, t=t):
                return e.reduce_sum(stats1[:, 4 * t:4 * t + 1], y_pre[:, t * 1024:(t + 1) * 1024],
                                    axis=mybir.AxisListType.X)
            OP("vector", f_sum1, deps=[], tag=f"sum1_{t}")

            def f_sq1(e, t=t):
                return e.activation(sq_s[:, :], y_pre[:, t * 1024:(t + 1) * 1024], AF.Square,
                                    accum_out=stats1[:, 4 * t + 1:4 * t + 2])
            tok_sq1[t] = OP("scalar", f_sq1, deps=[tok_rs1[t]], tag=f"sq1_{t}")

        tok_st1 = [None] * len(ST)
        for si, (t0, ntl) in enumerate(ST):
            def sviews(stats, t0=t0, ntl=ntl):
                sv = stats.rearrange("p (t k) -> p t k", k=4)
                return (sv[:, t0:t0 + ntl, 0:1], sv[:, t0:t0 + ntl, 1:2],
                        sv[:, t0:t0 + ntl, 2:3], sv[:, t0:t0 + ntl, 3:4])

            s_sum, s_sq, s_rstd, s_mbn = sviews(stats1)
            a = tmpa[:, t0:t0 + ntl]
            bq = tmpb[:, t0:t0 + ntl]
            cv = tmpc[:, t0:t0 + ntl]
            dep0 = tok_sq1[t0 + ntl - 1]
            OP("vector", lambda e, a=a, s_sum=s_sum: e.tensor_scalar_mul(a, s_sum, 1.0 / 1024.0),
               deps=[dep0], tag=f"st1.m{si}", fence=True)
            OP("vector", lambda e, bq=bq, s_sq=s_sq: e.tensor_scalar_mul(bq, s_sq, 1.0 / 1024.0),
               deps=[], tag=f"st1.q{si}", fence=True)
            OP("vector", lambda e, cv=cv, a=a: e.tensor_mul(cv, a, a),
               deps=[], tag=f"st1.2{si}", fence=True)
            OP("vector", lambda e, bq=bq, cv=cv: e.tensor_sub(bq, bq, cv),
               deps=[], tag=f"st1.s{si}", fence=True)
            OP("vector", lambda e, bq=bq: e.tensor_scalar_add(bq, bq, EPS),
               deps=[], tag=f"st1.e{si}", fence=True)
            ya = tmpd[:, t0:t0 + ntl]
            ta = tmpe[:, t0:t0 + ntl]
            emit_rsqrt(bq, ya, ta, None, f"st1.r{si}")
            OP("vector", lambda e, s_rstd=s_rstd, ya=ya: e.tensor_copy(s_rstd, ya),
               deps=[], tag=f"st1.cp{si}", fence=True)
            OP("vector", lambda e, cv=cv, a=a, ya=ya: e.tensor_mul(cv, a, ya),
               deps=[], tag=f"st1.mr{si}", fence=True)
            tok_st1[si] = OP("vector", lambda e, s_mbn=s_mbn, cv=cv: e.tensor_scalar_mul(s_mbn, cv, -1.0),
                             deps=[], tag=f"st1w{si}", fence=True)

    kb.barrier()
    dump("y_pre", y_pre[:, :])
    dump("stats1", stats1[:, :])

    with (
        nc.psum_tensor("ps_k", [P, 2 * 512], F32) as ps_k_h,
        nc.psum_tensor("ps_r", [P, 1024], F32) as ps_r_h,
        nc.psum_tensor("ps_v2", [P, 1024], F32) as ps_v2_h,
        nc.psum_tensor("ps_tr6", [P, 2 * 1024], BF16) as ps_tr6_h,
    ):
        ps_tr6 = ps_tr6_h.ap()
        ps_k = ps_k_h.ap()
        ps_r = ps_r_h.ap()
        ps_v2 = ps_v2_h.ap()

        stats1v = stats1.rearrange("p (t k) -> p t k", k=4)
        stats2v = stats2.rearrange("p (t k) -> p t k", k=4)
        yTv = yT.rearrange("p (c w) -> p c w", w=513)

        OP("scalar", lambda e: e.preload_act_table(AF.Sigmoid), tag="pre_sig")
        OP("scalar", lambda e: e.activation(tmpd[:, 0:1], mu_s[:, 0:1], AF.Sigmoid),
           tag="warm_sig")
        tok_mix = [None] * len(ST)
        tok_kact = [[None] * 8 for _ in range(len(ST))]
        tok_sig = [None] * NT
        tok_ff = [None] * NT
        tok_vm2 = [None] * NT
        tok_ytmp = [None] * NT
        tok_od = [None] * NT
        tok_kmm_last = [None] * len(ST)
        prev_last_ytr = None

        for si, (t0, ntl) in enumerate(ST):
            stw = ntl * 128
            tok_ybf = [None] * ntl
            for kk in range(ntl):
                t = t0 + kk

                def f_ybf(e, t=t, kk=kk):
                    return e.tensor_scalar(y_bfb[:, kk * 1024:(kk + 1) * 1024],
                                           y_pre[:, t * 1024:(t + 1) * 1024],
                                           stats1[:, 4 * t + 2:4 * t + 3],
                                           stats1[:, 4 * t + 3:4 * t + 4],
                                           ALU.mult, ALU.add)
                tok_ybf[kk] = OP("vector", f_ybf,
                                 deps=[tok_st1[si], prev_last_ytr],
                                 tag=f"ybf{t}")

            if si == 0:
                def f_carry(e):
                    return e.memset(yTv[:, :, 0], 0.0)
            else:
                def f_carry(e):
                    return e.tensor_copy(yTv[:, :, 0], yTv[:, :, 512])
            tok_carry = OP("vector", f_carry, deps=[tok_mix[si - 1] if si > 0 else None],
                           tag=f"carry{si}")

            tok_ytev = [None] * ntl
            tok_ytr = [None] * ntl
            for kk in range(ntl):
                slot = kk % 2

                def f_ytr(e, kk=kk, slot=slot):
                    ins = None
                    for c in range(8):
                        ins = e.transpose(ps_tr6[:, slot * 1024 + c * 128: slot * 1024 + (c + 1) * 128],
                                          y_bfb[:, kk * 1024 + c * 128: kk * 1024 + (c + 1) * 128],
                                          ident_s[:, :])
                    return ins
                tok_ytr[kk] = OP("tensor", f_ytr,
                                 deps=[tok_ybf[kk], tok_ytev[kk - 2] if kk >= 2 else None,
                                       tok_mix[si - 1] if si > 0 and kk == 0 else None],
                                 tag=f"ytr{si}.{kk}")

                def f_ytev(e, kk=kk, slot=slot):
                    dst = yTv[:, :, 1 + kk * 128: 1 + (kk + 1) * 128]
                    src = ps_tr6[:, slot * 1024:(slot + 1) * 1024].rearrange("p (c w) -> p c w", c=8)
                    return e.tensor_copy(dst, src)
                tok_ytev[kk] = OP("vector", f_ytev, deps=[tok_ytr[kk], tok_carry], tag=f"ytev{si}.{kk}")
            prev_last_ytr = tok_ytr[ntl - 1]

            def f_mix(e, si=si, stw=stw):
                ins = None
                for c in range(8):
                    yc = yTv[:, c, :]
                    e.tensor_sub(d1_s[:, 0:stw], yc[:, 1:1 + stw], yc[:, 0:stw])
                    e.scalar_tensor_tensor(xrT[:, c * 512: c * 512 + stw], d1_s[:, 0:stw],
                                           mu_s[:, c:c + 1], yc[:, 0:stw], ALU.mult, ALU.add)
                    ins = e.scalar_tensor_tensor(xkT[:, c * 512: c * 512 + stw], d1_s[:, 0:stw],
                                                 mu_s[:, 8 + c:9 + c], yc[:, 0:stw], ALU.mult, ALU.add)
                return ins
            tok_mix[si] = OP("vector", f_mix,
                             deps=[tok_vm2[t0 - 1] if si > 0 else None,
                                   tok_sig[t0 - 1] if si > 0 else None,
                                   t_mu],
                             tag=f"mix{si}")

            for dk in range(8):
                pslot = dk % 2

                def f_kmm(e, dk=dk, pslot=pslot, stw=stw):
                    ins = None
                    for cc in range(8):
                        ins = e.matmul(ps_k[:, pslot * 512: pslot * 512 + stw],
                                       w_k_b[:, cc * 1024 + dk * 128: cc * 1024 + dk * 128 + 128],
                                       xkT[:, cc * 512: cc * 512 + stw],
                                       start=(cc == 0), stop=(cc == 7))
                    return ins
                kdeps = [tok_mix[si], tok_kact[si][dk - 2] if dk >= 2 else None] + tok_wk
                if dk < 2 and si > 0:
                    kdeps.append(tok_kact[si - 1][6 + dk])
                if si > 0 and dk == 0:
                    kdeps.append(tok_vm2[t0 - 1])
                tok_kmm = OP("tensor", f_kmm, deps=kdeps, tag=f"kmm{si}.{dk}")

                def f_krelu(e, dk=dk, pslot=pslot, stw=stw):
                    return e.activation(tmp_bf[:, (dk % 2) * 512:(dk % 2) * 512 + stw],
                                        ps_k[:, pslot * 512: pslot * 512 + stw], AF.Relu)
                kreludeps = [tok_kmm]
                if dk >= 2:
                    kreludeps.append(tok_kact[si][dk - 2])
                elif si > 0:
                    kreludeps.append(tok_kact[si - 1][6 + dk])
                tok_krelu = OP("scalar", f_krelu, deps=kreludeps, tag=f"krelu{si}.{dk}")

                def f_kact(e, dk=dk, stw=stw):
                    tb = tmp_bf[:, (dk % 2) * 512:(dk % 2) * 512 + stw]
                    return e.tensor_mul(kact[:, dk * 512: dk * 512 + stw], tb, tb)
                tok_kact[si][dk] = OP("vector", f_kact, deps=[tok_krelu], tag=f"kact{si}.{dk}")
            tok_kmm_last[si] = tok_kact[si][7]

            for kk in range(ntl):
                t = t0 + kk

                def f_rmm(e, kk=kk, stw=stw):
                    ins = None
                    for cc in range(8):
                        lhsT = xrT[:, cc * 512 + kk * 128: cc * 512 + (kk + 1) * 128]
                        for half in range(2):
                            ins = e.matmul(ps_r[:, half * 512:(half + 1) * 512], lhsT,
                                           w_r_b[:, cc * 1024 + half * 512: cc * 1024 + (half + 1) * 512],
                                           start=(cc == 0), stop=(cc == 7))
                    return ins
                tok_rmm = OP("tensor", f_rmm,
                             deps=[tok_mix[si], tok_sig[t - 1] if t >= 1 else None] + tok_wr,
                             tag=f"rmm{t}")

                def f_sig(e):
                    return e.activation(r_bf[:, :], ps_r[:, :], AF.Sigmoid)
                tok_sig[t] = OP("scalar", f_sig,
                                deps=[tok_rmm, tok_ff[t - 1] if t >= 1 else None],
                                tag=f"sig{t}")

                def f_vm2(e, kk=kk):
                    ins = None
                    for dk in range(8):
                        lhsT = kact[:, dk * 512 + kk * 128: dk * 512 + (kk + 1) * 128]
                        for half in range(2):
                            ins = e.matmul(ps_v2[:, half * 512:(half + 1) * 512], lhsT,
                                           w_v_b[:, dk * 1024 + half * 512: dk * 1024 + (half + 1) * 512],
                                           start=(dk == 0), stop=(dk == 7))
                    return ins
                tok_vm2[t] = OP("tensor", f_vm2,
                                deps=[tok_kact[si][7], tok_ff[t - 1] if t >= 1 else None] + tok_wv2,
                                tag=f"vm2_{t}")

                def f_ytmp(e, t=t):
                    return e.tensor_scalar(ytmp[:, (t % 2) * 1024:(t % 2 + 1) * 1024],
                                           y_pre[:, t * 1024:(t + 1) * 1024],
                                           stats1[:, 4 * t + 2:4 * t + 3],
                                           stats1[:, 4 * t + 3:4 * t + 4],
                                           ALU.mult, ALU.add)
                tok_ytmp[t] = OP("vector", f_ytmp,
                                 deps=[tok_od[t - 2] if t >= 2 else None],
                                 tag=f"ytmp{t}")

                def f_ff(e, t=t):
                    return e.tensor_mul(ffn_s[:, :], r_bf[:, :], ps_v2[:, :])
                tok_ff[t] = OP("vector", f_ff, deps=[tok_sig[t], tok_vm2[t]], tag=f"ff{t}")

                def f_rs2(e, t=t):
                    yt = ytmp[:, (t % 2) * 1024:(t % 2 + 1) * 1024]
                    return e.tensor_add(yt, ffn_s[:, :], yt)
                tok_rs2 = OP("vector", f_rs2, deps=[], tag=f"rs2_{t}")

                def f_sum2(e, t=t):
                    return e.reduce_sum(stats2[:, 4 * t:4 * t + 1],
                                        ytmp[:, (t % 2) * 1024:(t % 2 + 1) * 1024],
                                        axis=mybir.AxisListType.X)
                OP("vector", f_sum2, deps=[], tag=f"sum2_{t}")

                def f_sq2(e, t=t):
                    return e.activation(sq_s[:, :], ytmp[:, (t % 2) * 1024:(t % 2 + 1) * 1024],
                                        AF.Square, accum_out=stats2[:, 4 * t + 1:4 * t + 2])
                tok_sq2 = OP("scalar", f_sq2, deps=[tok_rs2], tag=f"sq2_{t}")

                a = tmpa[:, t:t + 1]
                bq = tmpb[:, t:t + 1]
                cv = tmpc[:, t:t + 1]
                s2s = stats2[:, 4 * t:4 * t + 1]
                s2q = stats2[:, 4 * t + 1:4 * t + 2]
                s2r = stats2[:, 4 * t + 2:4 * t + 3]
                s2m = stats2[:, 4 * t + 3:4 * t + 4]
                OP("vector", lambda e, a=a, s2s=s2s: e.tensor_scalar_mul(a, s2s, 1.0 / 1024.0),
                   deps=[tok_sq2], tag=f"ln2.m{t}", fence=True)
                OP("vector", lambda e, bq=bq, s2q=s2q: e.tensor_scalar_mul(bq, s2q, 1.0 / 1024.0),
                   deps=[], tag=f"ln2.q{t}", fence=True)
                OP("vector", lambda e, cv=cv, a=a: e.tensor_mul(cv, a, a),
                   deps=[], tag=f"ln2.2{t}", fence=True)
                OP("vector", lambda e, bq=bq, cv=cv: e.tensor_sub(bq, bq, cv),
                   deps=[], tag=f"ln2.s{t}", fence=True)
                OP("vector", lambda e, bq=bq: e.tensor_scalar_add(bq, bq, EPS),
                   deps=[], tag=f"ln2.e{t}", fence=True)
                ya = tmpd[:, t:t + 1]
                ta = tmpe[:, t:t + 1]
                emit_rsqrt(bq, ya, ta, None, f"ln2.r{t}")
                tok_ln2s = OP("vector", lambda e, s2r=s2r, ya=ya: e.tensor_copy(s2r, ya),
                              deps=[], tag=f"ln2.cp{t}", fence=True)
                OP("vector", lambda e, cv=cv, a=a, ya=ya: e.tensor_mul(cv, a, ya),
                   deps=[], tag=f"ln2.mr{t}", fence=True)
                tok_ln2w = OP("vector", lambda e, s2m=s2m, cv=cv: e.tensor_scalar_mul(s2m, cv, -1.0),
                              deps=[], tag=f"ln2w{t}", fence=True)

                def f_outv(e, t=t):
                    yt = ytmp[:, (t % 2) * 1024:(t % 2 + 1) * 1024]
                    return e.tensor_scalar(yt, yt, stats2[:, 4 * t + 2:4 * t + 3],
                                           stats2[:, 4 * t + 3:4 * t + 4],
                                           ALU.mult, ALU.add)
                tok_outv = OP("vector", f_outv, deps=[tok_ln2s], tag=f"outv{t}")

                def f_od(e, t=t):
                    return e.dma_start(out=out_h[t * 128:(t + 1) * 128, :],
                                       in_=ytmp[:, (t % 2) * 1024:(t % 2 + 1) * 1024])
                tok_od[t] = OP("sync", f_od, deps=[tok_outv], tag=f"od{t}")

    kb.final_wait("sync")

    sems = {}
    sem_ctxs = []
    for e in ENGINES:
        h = nc.semaphore(f"sem_{e}")
        sem_ctxs.append(h)
        sems[e] = h.__enter__()
    hbar = nc.semaphore("sem_bar")
    sem_ctxs.append(hbar)
    sem_bar = hbar.__enter__()

    body_for = kb.emit(nc, sems, sem_bar)
    with nc.Block() as blk:
        blk.tensor(body_for("tensor"))
        blk.vector(body_for("vector"))
        blk.scalar(body_for("scalar"))
        blk.sync(body_for("sync"))
        blk.gpsimd(body_for("gpsimd"))

    return nc


_NC_CACHE = None


def kernel(L0_attn_query, L0_attn_in_proj_w, L0_attn_out_proj_w,
           L0_ffn_W_r, L0_ffn_W_k, L0_ffn_W_v, L0_ffn_mu_r, L0_ffn_mu_k):
    global _NC_CACHE
    args = [np.asarray(a) for a in (L0_attn_query, L0_attn_in_proj_w, L0_attn_out_proj_w,
                                    L0_ffn_W_r, L0_ffn_W_k, L0_ffn_W_v, L0_ffn_mu_r, L0_ffn_mu_k)]
    in_maps = host_prep(*args)
    if _NC_CACHE is None:
        _NC_CACHE = build_program()
    nc = _NC_CACHE
    res = run_bass_kernel_spmd(nc, in_maps, core_ids=list(range(8)))
    out = np.empty((B, T, E), dtype=np.float32)
    for b in range(B):
        out[b, 0:CT] = res.results[2 * b]["out_y"]
        out[b, CT:T] = res.results[2 * b + 1]["out_y"][2 * 128:][: T - CT]
    return out
